# version 01 (final)
# baseline (speedup 1.0000x reference)
import numpy as np
import ml_dtypes

import concourse.bass as bass
import concourse.mybir as mybir
from concourse.bass_utils import run_bass_kernel_spmd

F32 = mybir.dt.float32
BF16 = mybir.dt.bfloat16
AF = mybir.ActivationFunctionType
ALU = mybir.AluOpType
X_AX = mybir.AxisListType.X
bf16 = ml_dtypes.bfloat16

D = 1024
NT = 17
NCOL = NT * 128
TB = 8192
NFT = 64
NE = 32
H_B = 4
NCH = 16
CHUNKS = [(0, 512), (512, 512), (1024, 512), (1536, 512), (2048, 128)]
HPAIRS = [(0, 4), (1, 5), (2, 6), (3, 7), (8, 12), (9, 13), (10, 14), (11, 15)]
QPERM = [h for pr in HPAIRS for h in pr]

IN_NAMES_A = ["x_loc", "x_full", "wq_a", "wk_a", "wv_a", "wo_a", "guT", "dnT", "sgT",
              "suT", "sdT", "gateT", "bias_bc", "wq_b", "wk_b", "wv_b", "cw",
              "stril", "striu", "triu_i", "i256", "ident", "hselq", "hselk"]


class Em:
    ENGINES = ("sync", "tensor", "vector", "scalar", "gpsimd")

    def __init__(self, nc):
        self.nc = nc
        self.q = {e: [] for e in self.ENGINES}
        self.sems = {}

    def s(self, name):
        if name not in self.sems:
            self.sems[name] = [self.nc.alloc_semaphore(name), 0]
        return self.sems[name]

    def val(self, name):
        return self.s(name)[1]

    def op(self, eng, build, wait=(), inc=None, inc_by=1):
        waits = []
        for n, v in wait:
            if v <= 0:
                continue
            sh = self.s(n)
            assert v <= sh[1], f"wait {n}>={v} but only {sh[1]} emitted ({eng})"
            waits.append((sh[0], v))
        incinfo = None
        if inc is not None:
            sh = self.s(inc)
            sh[1] += inc_by
            incinfo = (sh[0], inc_by)

        def f(e, build=build, waits=waits, incinfo=incinfo):
            for h, v in waits:
                e.wait_ge(h, v)
            ins = build(e)
            if incinfo is not None:
                ins.then_inc(incinfo[0], incinfo[1])
        self.q[eng].append(f)
        return self.val(inc) if inc is not None else None

    def barrier(self):
        tgt = self.val("bar") + len(self.ENGINES)
        for eng in self.ENGINES:
            self.op(eng, lambda e: e.drain(), inc="bar")
        for eng in self.ENGINES:
            self.op(eng, lambda e: e.nop(), wait=[("bar", tgt)])

    def finalize(self):
        with self.nc.allow_low_precision("bf16 matmul operands by design"), \
                self.nc.Block() as block:
            for eng in self.ENGINES:
                def body(e, eng=eng):
                    for f in self.q[eng]:
                        f(e)
                getattr(block, eng)(body)



def host_stage(inputs):
    f32 = np.float32
    x = np.asarray(inputs["L0_attn_a_x"], f32)

    def wchunks(w):
        K = w.shape[0]
        return np.ascontiguousarray(w.reshape(K // 128, 128, -1).transpose(1, 0, 2))

    sh = {}
    wq_raw = np.asarray(inputs["L0_attn_a_Wq"], f32)
    sh["wq_a"] = wchunks(np.ascontiguousarray(
        wq_raw.reshape(1024, 16, 64)[:, QPERM, :].reshape(1024, 1024)))
    sh["wk_a"] = wchunks(np.asarray(inputs["L0_attn_a_Wk"], f32))
    sh["wv_a"] = wchunks(np.asarray(inputs["L0_attn_a_Wv"], f32))
    wo_raw = np.asarray(inputs["L0_attn_a_Wo"], f32)
    sh["wo_a"] = wchunks(np.ascontiguousarray(
        wo_raw.reshape(16, 64, 1024)[QPERM].reshape(1024, 1024)))
    gu = np.asarray(inputs["L0_ffn_experts_gate_up"], f32)
    dn = np.asarray(inputs["L0_ffn_experts_down"], f32)
    sh["guT"] = np.ascontiguousarray(
        gu.transpose(0, 2, 1).reshape(NE, 8, 128, 1024).transpose(0, 2, 1, 3))
    sh["dnT"] = np.ascontiguousarray(
        dn.transpose(0, 2, 1).reshape(NE, 4, 128, 1024).transpose(0, 2, 1, 3))
    sh["sgT"] = wchunks(np.ascontiguousarray(np.asarray(inputs["L0_ffn_shared_gate_w"], f32).T))
    sh["suT"] = wchunks(np.ascontiguousarray(np.asarray(inputs["L0_ffn_shared_up_w"], f32).T))
    sh["sdT"] = wchunks(np.ascontiguousarray(np.asarray(inputs["L0_ffn_shared_down_w"], f32).T))
    sh["gateT"] = wchunks(np.ascontiguousarray(np.asarray(inputs["L0_ffn_gate_weight"], f32).T))
    sh["bias_bc"] = np.ascontiguousarray(np.broadcast_to(
        np.asarray(inputs["L0_ffn_e_score_correction_bias"], f32)[None, :], (128, 32)))
    sh["wq_b"] = wchunks(np.asarray(inputs["L0_attn_b_Wq"], f32))
    sh["wk_b"] = wchunks(np.asarray(inputs["L0_attn_b_Wk"], f32))
    sh["wv_b"] = wchunks(np.asarray(inputs["L0_attn_b_Wv"], f32))
    sh["wo_b"] = wchunks(np.asarray(inputs["L0_attn_b_Wo"], f32))
    sh["cw"] = np.ascontiguousarray(np.stack([
        np.asarray(inputs["L0_attn_b_conv_wq"], f32),
        np.asarray(inputs["L0_attn_b_conv_wk"], f32),
        np.asarray(inputs["L0_attn_b_conv_wv"], f32),
    ]).reshape(3, 8, 128, 4).transpose(0, 2, 1, 3))
    sh["wrep"] = np.ascontiguousarray(
        np.tile(np.asarray(inputs["L0_attn_b_onorm_w"], f32), 4).reshape(8, 128).T)

    ones = np.ones((128, 128), f32)
    sh["stril"] = np.tril(ones, -1).astype(bf16)
    sh["striu"] = np.triu(ones, 1).astype(bf16)
    sh["triu_i"] = np.triu(ones, 0).astype(bf16)
    i256 = np.zeros((128, 2, 256), f32)
    for c in range(2):
        i256[np.arange(128), c, c * 128 + np.arange(128)] = 1.0
    sh["i256"] = i256.astype(bf16)
    sh["ident"] = np.eye(128, dtype=f32).astype(bf16)
    hq = np.zeros((128, 8, 16), f32)
    for dc in range(8):
        hq[np.arange(128), dc, 2 * dc + np.arange(128) // 64] = 1.0
    sh["hselq"] = hq.astype(bf16)
    hk = np.zeros((128, 2, 4), f32)
    for kc in range(2):
        hk[np.arange(128), kc, 2 * kc + np.arange(128) // 64] = 1.0
    sh["hselk"] = hk.astype(bf16)

    per_core = []
    for c in range(8):
        b, r = c // 4, c % 4
        d = dict(sh)
        xb = x[b]
        d["x_full"] = np.ascontiguousarray(xb.reshape(64, 128, 1024).transpose(1, 0, 2))
        xl = np.zeros((17, 128, 1024), f32)
        xl[1:] = xb[2048 * r: 2048 * (r + 1)].reshape(16, 128, 1024)
        if r > 0:
            xl[0, 125:128] = xb[2048 * r - 3: 2048 * r]
        d["x_loc"] = np.ascontiguousarray(xl.transpose(1, 0, 2))
        g = np.zeros((128, 3), f32)
        g[:, :r] = 1.0
        d["gates"] = g
        per_core.append(d)
    return per_core



def emit_tile_ln_t(em, src_tile, dst_slice, lnsc, pA, ident):
    xln = lnsc["xln"]
    m, s2, t2 = lnsc["m"], lnsc["s2"], lnsc["t2"]
    def dve(b):
        em.op("vector", b, wait=[("dv", em.val("dv"))], inc="dv")
    dve(lambda e: e.reduce_sum(m[:], src_tile, X_AX))
    em.op("vector", lambda e: e.tensor_mul(xln[:, 0, :], src_tile, src_tile),
          wait=[("pe", em.val("pe")), ("dv", em.val("dv"))], inc="dv")
    dve(lambda e: e.reduce_sum(s2[:], xln[:, 0, :], X_AX))
    dve(lambda e: e.tensor_scalar(m[:], m[:], 1.0 / D, None, ALU.mult))
    dve(lambda e: e.tensor_tensor(t2[:], m[:], m[:], ALU.mult))
    dve(lambda e: e.tensor_scalar(s2[:], s2[:], 1.0 / D, None, ALU.mult))
    dve(lambda e: e.tensor_tensor(s2[:], s2[:], t2[:], ALU.subtract))
    em.op("scalar", lambda e: e.activation(t2[:], s2[:], AF.Sqrt, bias=lnsc["eps"][:]),
          wait=[("dv", em.val("dv"))], inc="ac")
    dve_w = [("ac", em.val("ac")), ("dv", em.val("dv"))]
    em.op("vector", lambda e: e.reciprocal(s2[:], t2[:]), wait=dve_w, inc="dv")
    dve(lambda e: e.tensor_scalar(xln[:, 0, :], src_tile, m[:],
                                  s2[:], ALU.subtract, ALU.mult))
    ptv = pA.ap().bitcast(BF16)
    for c in range(8):
        em.op("tensor", lambda e, c=c: e.transpose(
            ptv[:, c * 128:(c + 1) * 128], xln[:, 0, c * 128:(c + 1) * 128], ident[:]),
            wait=([("dv", em.val("dv")), ("ac", em.val("ac"))] if c == 0 else ()),
            inc=("pe" if c == 7 else None))
    em.op("vector", lambda e: e.tensor_copy(
        dst_slice, ptv[:, 0:1024].rearrange("p (c w) -> p c w", c=8)),
          wait=[("pe", em.val("pe"))], inc="dv")


def emit_moe(em, ap, x, xlnT, B, n_exp=NE, do_shared=True):
    pA, pB, p6 = B["pA"], B["pB"], B["p6"]
    gateT, bias_bc = B["gateT"], B["bias_bc"]
    comb, rsc = B["comb"], B["rsc"]
    gub, dnb = B["gub"], B["dnb"]
    hprime2, sg = B["hprime"], B["sg"]

    for t in range(NT):
        for h in range(8):
            em.op("tensor", lambda e, t=t, h=h: e.matmul(
                p6[0:128, 0:32], xlnT[:, h, t * 128:(t + 1) * 128], gateT[:, h, :],
                start=(h == 0), stop=(h == 7)),
                wait=([("ac", em.val("ac"))] if h == 0 else ()),
                inc=("pe" if h == 7 else None))
        em.op("scalar", lambda e: e.activation(rsc[:, 0, 0:32], p6[0:128, 0:32], AF.Sigmoid),
              wait=[("pe", em.val("pe")), ("dv", em.val("dv"))], inc="ac")

        def dve(b, extra=()):
            em.op("vector", b, wait=[("dv", em.val("dv"))] + list(extra), inc="dv")
        dve(lambda e: e.tensor_tensor(rsc[:, 1, 0:32], rsc[:, 0, 0:32], bias_bc[:],
                                      ALU.add), extra=[("ac", em.val("ac"))])
        v = rsc[:, 1, 0:32].rearrange("p (g s) -> p g s", g=8)
        dve(lambda e, v=v: e.tensor_tensor(rsc[:, 2, 0:8], v[:, :, 0], v[:, :, 1], ALU.max))
        dve(lambda e, v=v: e.tensor_tensor(rsc[:, 2, 8:16], v[:, :, 2], v[:, :, 3], ALU.max))
        dve(lambda e: e.tensor_tensor(rsc[:, 2, 0:8], rsc[:, 2, 0:8], rsc[:, 2, 8:16], ALU.max))
        dve(lambda e: e.memset(rsc[:, 3, 0:8], 0.0))
        for _ in range(4):
            dve(lambda e: e.reduce_max(rsc[:, 4, 0:1], rsc[:, 2, 0:8], X_AX))
            dve(lambda e: e.tensor_scalar(rsc[:, 4, 8:16], rsc[:, 2, 0:8],
                                          rsc[:, 4, 0:1], None, ALU.is_ge))
            dve(lambda e: e.tensor_tensor(rsc[:, 3, 0:8], rsc[:, 3, 0:8],
                                          rsc[:, 4, 8:16], ALU.add))
            dve(lambda e: e.scalar_tensor_tensor(rsc[:, 2, 0:8], rsc[:, 4, 8:16],
                                                 -1e5, rsc[:, 2, 0:8], ALU.mult, ALU.add))
        dve(lambda e: e.tensor_scalar_min(rsc[:, 3, 0:8], rsc[:, 3, 0:8], 1.0))
        dve(lambda e: e.tensor_copy(
            rsc[:, 4, 0:32].rearrange("p (g s) -> p g s", g=8),
            rsc[:, 3, 0:8][:, :, None].broadcast_to([128, 8, 4])))
        dve(lambda e: e.tensor_tensor(rsc[:, 2, 0:32], rsc[:, 1, 0:32], rsc[:, 4, 0:32],
                                      ALU.mult))
        dve(lambda e: e.memset(rsc[:, 3, 0:32], 0.0))
        for _ in range(6):
            dve(lambda e: e.reduce_max(rsc[:, 4, 0:1], rsc[:, 2, 0:32], X_AX))
            dve(lambda e: e.tensor_scalar(rsc[:, 4, 0:32], rsc[:, 2, 0:32],
                                          rsc[:, 4, 0:1], None, ALU.is_ge))
            dve(lambda e: e.tensor_tensor(rsc[:, 3, 0:32], rsc[:, 3, 0:32],
                                          rsc[:, 4, 0:32], ALU.add))
            dve(lambda e: e.scalar_tensor_tensor(rsc[:, 2, 0:32], rsc[:, 4, 0:32],
                                                 -1e5, rsc[:, 2, 0:32], ALU.mult, ALU.add))
        dve(lambda e: e.tensor_scalar_min(rsc[:, 3, 0:32], rsc[:, 3, 0:32], 1.0))
        dve(lambda e: e.tensor_tensor(rsc[:, 1, 0:32], rsc[:, 0, 0:32], rsc[:, 3, 0:32],
                                      ALU.mult))
        dve(lambda e: e.reduce_sum(rsc[:, 4, 0:1], rsc[:, 1, 0:32], X_AX))
        dve(lambda e: e.tensor_scalar(rsc[:, 4, 0:1], rsc[:, 4, 0:1], 1e-20, None, ALU.add))
        dve(lambda e: e.reciprocal(rsc[:, 4, 1:2], rsc[:, 4, 0:1]))
        dve(lambda e, t=t: e.tensor_scalar(comb[:, t, :], rsc[:, 1, 0:32],
                                           rsc[:, 4, 1:2], 2.5, ALU.mult, ALU.mult))
    comb_dv = em.val("dv")

    pe_mm_done = {}
    slots = list(range(n_exp)) + ([NE, NE + 1] if do_shared else [])
    for ei in slots:
        j = ei % 2
        shared = (ei >= NE)
        wfree = [("pe", pe_mm_done.get(ei - 2, 0))]
        if not shared:
            em.op("gpsimd", lambda e, ei=ei, j=j: e.dma_start(out=gub[j][:], in_=ap["guT"][ei]),
                  wait=wfree, inc="ldw", inc_by=16)
            em.op("gpsimd", lambda e, ei=ei, j=j: e.dma_start(out=dnb[j][:], in_=ap["dnT"][ei]),
                  wait=wfree, inc="ldw", inc_by=16)
        else:
            half = ei - NE
            em.op("gpsimd", lambda e, j=j, half=half: e.dma_start(
                out=gub[j][:, :, 0:512], in_=ap["sgT"][:, :, 512 * half:512 * half + 512]),
                wait=wfree, inc="ldw", inc_by=16)
            em.op("gpsimd", lambda e, j=j, half=half: e.dma_start(
                out=gub[j][:, :, 512:1024], in_=ap["suT"][:, :, 512 * half:512 * half + 512]),
                wait=wfree, inc="ldw", inc_by=16)
            em.op("gpsimd", lambda e, j=j, half=half: e.dma_start(
                out=dnb[j][:], in_=ap["sdT"][:, 4 * half:4 * half + 4, :]),
                wait=wfree, inc="ldw", inc_by=16)
        ldw_done = em.val("ldw")
        hprime = hprime2[j]
        hp_first = True

        for ip in range(4):
            for (c0, cn) in CHUNKS:
                for h in range(8):
                    lg = gub[j][:, h, ip * 128:(ip + 1) * 128]
                    lu = gub[j][:, h, 512 + ip * 128: 512 + (ip + 1) * 128]
                    w0 = [("ldw", ldw_done), ("dv", em.val("dv"))] if h == 0 else ()
                    em.op("tensor", lambda e, lg=lg, c0=c0, cn=cn, h=h: e.matmul(
                        pA[:, 0:cn], lg, xlnT[:, h, c0:c0 + cn],
                        start=(h == 0), stop=(h == 7)), wait=w0)
                    em.op("tensor", lambda e, lu=lu, c0=c0, cn=cn, h=h: e.matmul(
                        pA[:, 512:512 + cn], lu, xlnT[:, h, c0:c0 + cn],
                        start=(h == 0), stop=(h == 7)),
                        inc=("pe" if h == 7 else None))
                em.op("scalar", lambda e, cn=cn: e.activation(sg[:, 0:cn], pA[:, 0:cn], AF.Silu),
                      wait=[("pe", em.val("pe")), ("dv", em.val("dv"))], inc="ac")
                wtt = [("ac", em.val("ac"))]
                if hp_first:
                    wtt.append(("pe", pe_mm_done.get(ei - 2, 0)))
                    hp_first = False
                em.op("vector", lambda e, ip=ip, c0=c0, cn=cn: e.tensor_tensor(
                    hprime[:, ip, c0:c0 + cn], sg[:, 0:cn], pA[:, 512:512 + cn], ALU.mult),
                    wait=wtt, inc="dv")
        hp_dv = em.val("dv")

        dv_evac = {0: 0, 1: 0}
        first_mm = True
        for t in range(NT):
            for dh in range(2):
                par = (t * 2 + dh) % 2
                pp = pB[:, 0:512] if par == 0 else pB[:, 512:1024]
                for i in range(4):
                    rw = dnb[j][:, i, dh * 512:(dh + 1) * 512]
                    w0 = []
                    if first_mm:
                        w0 = [("dv", max(hp_dv, comb_dv))]
                    elif i == 0:
                        w0 = [("dv", dv_evac[par])]
                    em.op("tensor", lambda e, i=i, t=t, rw=rw, pp=pp: e.matmul(
                        pp, hprime[:, i, t * 128:(t + 1) * 128], rw,
                        start=(i == 0), stop=(i == 3)),
                        wait=w0, inc=("pe" if i == 3 else None))
                    first_mm = False
                pe_now = em.val("pe")
                if shared:
                    em.op("vector", lambda e, t=t, dh=dh, pp=pp: e.tensor_tensor(
                        x[:, t, dh * 512:(dh + 1) * 512], x[:, t, dh * 512:(dh + 1) * 512],
                        pp, ALU.add), wait=[("pe", pe_now)], inc="dv")
                else:
                    em.op("vector", lambda e, t=t, dh=dh, pp=pp, ei=ei: e.scalar_tensor_tensor(
                        x[:, t, dh * 512:(dh + 1) * 512], pp, comb[:, t, ei:ei + 1],
                        x[:, t, dh * 512:(dh + 1) * 512], ALU.mult, ALU.add),
                        wait=[("pe", pe_now)], inc="dv")
                dv_evac[par] = em.val("dv")
        pe_mm_done[ei] = em.val("pe")
    return em.val("dv")



def build_launch_A(dbg=None, nft=NFT, kvbody=4, n_exp=NE, do_shared=True, nch=NCH, q3=4):
    nc = bass.Bass("TRN2", target_bir_lowering=False, detect_race_conditions=False)
    ap = {}

    def IN(name, shape, dt=F32):
        ap[name] = nc.dram_tensor(name, shape, dt, kind="ExternalInput").ap()

    def OUT(name, shape, dt=F32):
        ap[name] = nc.dram_tensor(name, shape, dt, kind="ExternalOutput").ap()

    IN("x_loc", [128, NT, D]); IN("x_full", [128, NFT, D])
    IN("wq_a", [128, 8, 1024]); IN("wk_a", [128, 8, 256]); IN("wv_a", [128, 8, 256])
    IN("wo_a", [128, 8, 1024])
    IN("guT", [NE, 128, 8, 1024]); IN("dnT", [NE, 128, 4, 1024])
    IN("sgT", [128, 8, 1024]); IN("suT", [128, 8, 1024]); IN("sdT", [128, 8, 1024])
    IN("gateT", [128, 8, 32]); IN("bias_bc", [128, 32])
    IN("wq_b", [128, 8, 1024]); IN("wk_b", [128, 8, 1024]); IN("wv_b", [128, 8, 1024])
    IN("cw", [3, 128, 8, 4])
    IN("stril", [128, 128], BF16); IN("striu", [128, 128], BF16); IN("triu_i", [128, 128], BF16)
    IN("i256", [128, 2, 256], BF16); IN("ident", [128, 128], BF16)
    IN("hselq", [128, 8, 16], BF16); IN("hselk", [128, 2, 4], BF16)

    OUT("x2_out", [128, NT, D])
    if dbg == "x2":
        OUT("dbg_comb", [128, NT, 32])
        OUT("dbg_rsc", [128, 5, 32])
    OUT("spill_Q", [128, 8, 2048], BF16)
    OUT("spill_U0", [H_B, NCH, 128, 256], BF16)
    OUT("spill_WT", [H_B, NCH, 128, 2, 128], BF16)
    OUT("spill_K", [H_B, NCH, 128, 256], BF16)
    OUT("spill_TT", [H_B, NCH, 128, 128], BF16)
    OUT("maps_PT", [H_B, 128, 2, 256], BF16)
    OUT("maps_B", [H_B, 128, 2, 256], BF16)
    if dbg == "x1":
        OUT("dbg_x1", [128, NT, D])
    if dbg == "init":
        OUT("dbg_x0", [128, NT, D])
    if dbg == "ln1":
        OUT("dbg_ln", [128, 8, 128], BF16)
        OUT("dbg_m", [128, 1])
        OUT("dbg_s2", [128, 1])
        OUT("dbg_xln", [128, 1, D], BF16)
    if dbg == "kv":
        OUT("dbg_kT", [128, 2, TB], BF16)
        OUT("dbg_vaug", [128, NFT, 4, 65], BF16)
        OUT("dbg_cstab", [1, 4])

    em = Em(nc)
    A = nc.alloc_sbuf_tensor
    PS = nc.alloc_psum_tensor

    top0 = nc.sbuf_top
    x = A("x", [128, NT, D], F32, side="right")
    ident = A("s_ident", [128, 128], BF16)
    ones1_128 = A("ones1_128", [1, 128], BF16)
    ones128_1 = A("ones128_1", [128, 1], BF16)
    ones1_64 = A("ones1_64", [1, 64], BF16)
    lnsc = {"m": A("ln_m", [128, 1], F32), "s2": A("ln_s2", [128, 1], F32),
            "t2": A("ln_t2", [128, 1], F32),
            "xln": A("ln_xln", [128, 1, D], BF16),
            "eps": A("ln_eps", [128, 1], F32)}

    pA = PS("pA", [128, 1024], F32)
    pB = PS("pB", [128, 1024], F32)
    p4 = PS("p4", [128, 512], F32)
    p5 = PS("p5", [128, 512], F32)
    p6 = PS("p6", [128, 512], F32)
    p7 = PS("p7", [128, 512], F32)

    def dma(eng, dst, src, sem="ldx"):
        return em.op(eng, lambda e, dst=dst, src=src: e.dma_start(out=dst, in_=src),
                     inc=sem, inc_by=16)

    dma("sync", x[:], ap["x_loc"])
    dma("sync", ident[:], ap["ident"])
    ldx0 = em.val("ldx")
    em.op("vector", lambda e: e.memset(ones1_128[:], 1.0), wait=[("ldx", ldx0)])
    em.op("vector", lambda e: e.memset(ones128_1[:], 1.0))
    em.op("vector", lambda e: e.memset(ones1_64[:], 1.0))
    em.op("vector", lambda e: e.memset(lnsc["eps"][:], 1e-5), inc="dv")
    em.op("tensor", lambda e: e.nop(), wait=[("ldx", ldx0)])
    em.op("scalar", lambda e: e.nop(), wait=[("ldx", ldx0)])

    markA = nc.sbuf_base
    wq = A("wq", [128, 8, 1024], BF16); wk = A("wk", [128, 8, 256], BF16)
    wv = A("wv", [128, 8, 256], BF16); wo = A("wo", [128, 8, 1024], BF16)
    hselk = A("hselk_s", [128, 2, 4], BF16); hselq = A("hselq_s", [128, 8, 16], BF16)
    kT = A("kT", [128, 2, TB], BF16)
    vaug = A("vaug", [128, NFT, 4, 65], BF16)
    xf_t = A("xf_t", [128, 1, D], F32)
    xlnT_t = A("xlnT_t", [128, 8, 128], BF16)
    kn2 = A("kn2", [4, 2], F32)
    sqs = A("sqs", [128, 128], BF16)
    cstab = A("cstab", [1, 4], F32)
    small_bf = A("small_bf", [16, 1], BF16)
    rowscr = A("rowscr", [1, 16], F32)
    qch = A("qch", [128, 8, 512], BF16)
    oT = A("oT", [128, 8, 512], BF16)
    pbuf = A("pbuf", [128, 2, 1024], BF16)
    biasc = A("biasc", [128, 1], F32)
    qn_f = A("qn_f", [16, 1], F32)
    rrow = A("rrow", [1, 512], BF16)
    otmp = A("otmp", [64, 512], BF16)
    cbf = A("cbf", [1, 1], BF16)

    for nm, t in [("wq_a", wq), ("wk_a", wk), ("wv_a", wv), ("wo_a", wo)]:
        dma("gpsimd", t[:], ap[nm], "ldw")
    dma("sync", hselk[:], ap["hselk"])
    dma("sync", hselq[:], ap["hselq"])
    em.op("tensor", lambda e: e.nop(), wait=[("ldw", em.val("ldw")), ("ldx", em.val("ldx"))])
    em.op("scalar", lambda e: e.nop(), wait=[("ldw", em.val("ldw")), ("ldx", em.val("ldx"))])

    em.op("vector", lambda e: e.memset(kn2[:, 0:1], -3e38))
    em.op("vector", lambda e: e.memset(cstab[:], 0.0), inc="dv")

    if dbg == "init":
        em.op("gpsimd", lambda e: e.dma_start(out=ap["dbg_x0"], in_=x[:]),
              wait=[("dv", em.val("dv"))], inc="st", inc_by=16)
        em.op("gpsimd", lambda e: e.nop(), wait=[("st", em.val("st"))])
        em.finalize()
        return nc
    if dbg == "ln1":
        emit_tile_ln_t(em, x[:, 1, :], xlnT_t[:, :, :], lnsc, pA, ident)
        em.op("gpsimd", lambda e: e.dma_start(out=ap["dbg_ln"], in_=xlnT_t[:]),
              wait=[("dv", em.val("dv"))], inc="st", inc_by=16)

        em.op("gpsimd", lambda e: e.dma_start(out=ap["dbg_xln"], in_=lnsc["xln"][:]),
              inc="st", inc_by=16)
        em.op("gpsimd", lambda e: e.dma_start(out=ap["dbg_m"], in_=lnsc["m"][:]),
              inc="st", inc_by=16)
        em.op("gpsimd", lambda e: e.dma_start(out=ap["dbg_s2"], in_=lnsc["s2"][:]),
              inc="st", inc_by=16)
        em.op("gpsimd", lambda e: e.nop(), wait=[("st", em.val("st"))])
        em.finalize()
        return nc

    xf_free = ("dv", 0)
    for ft in range(nft):
        em.op("sync", lambda e, ft=ft: e.dma_start(out=xf_t[:, 0, :], in_=ap["x_full"][:, ft, :]),
              wait=[xf_free], inc="ldx", inc_by=16)
        ld_now = em.val("ldx")
        em.op("vector", lambda e: e.nop(), wait=[("ldx", ld_now)])
        em.op("scalar", lambda e: e.nop(), wait=[("ldx", ld_now)])
        emit_tile_ln_t(em, xf_t[:, 0, :], xlnT_t[:, :, :], lnsc, pA, ident)
        xf_free = ("dv", em.val("dv"))
        if kvbody < 1:
            continue
        for kc in range(2):
            for h in range(8):
                em.op("tensor", lambda e, kc=kc, h=h: e.matmul(
                    p4[:, 0:128], wk[:, h, kc * 128:(kc + 1) * 128], xlnT_t[:, h, :],
                    start=(h == 0), stop=(h == 7)),
                    wait=([("dv", em.val("dv")), ("ac", em.val("ac"))] if h == 0 else ()),
                    inc=("pe" if h == 7 else None))
            pe_now = em.val("pe")
            em.op("vector", lambda e, kc=kc, ft=ft: e.tensor_copy(
                kT[:, kc, ft * 128:(ft + 1) * 128], p4[:, 0:128]),
                wait=[("pe", pe_now)], inc="dv")
            if kvbody < 2:
                continue
            em.op("scalar", lambda e: e.activation(sqs[:, 0:128], p4[:, 0:128], AF.Square),
                  wait=[("pe", pe_now), ("dv", em.val("dv"))], inc="ac")
            if kvbody == 21:
                continue
            em.op("tensor", lambda e, kc=kc: e.matmul(
                p5[0:4, 0:128], hselk[:, kc, :], sqs[:, 0:128], start=True, stop=True),
                wait=[("ac", em.val("ac")), ("dv", em.val("dv"))], inc="pe")
            if kvbody == 22:
                continue
            em.op("vector", lambda e: e.reduce_max(kn2[:, 1:2], p5[0:4, 0:128], X_AX),
                  wait=[("pe", em.val("pe")), ("dv", em.val("dv"))], inc="dv")
            em.op("vector", lambda e: e.tensor_tensor(kn2[:, 0:1], kn2[:, 0:1], kn2[:, 1:2],
                                                      ALU.max), wait=[("dv", em.val("dv"))],
                  inc="dv")
        if kvbody < 3 or kvbody > 20:
            continue
        for h in range(8):
            em.op("tensor", lambda e, h=h: e.matmul(
                p6[:, 0:256], xlnT_t[:, h, :], wv[:, h, :], start=(h == 0), stop=(h == 7)),
                wait=([("dv", em.val("dv"))] if h == 0 else ()),
                inc=("pe" if h == 7 else None))
        em.op("vector", lambda e, ft=ft: e.tensor_copy(
            vaug[:, ft, :, 0:64], p6[:, 0:256].rearrange("p (g w) -> p g w", g=4)),
            wait=[("pe", em.val("pe"))], inc="dv")
    em.op("vector", lambda e: e.memset(vaug[:, :, :, 64:65], 1.0), inc="dv")
    em.op("vector", lambda e: e.tensor_copy(small_bf[0:4, :], kn2[:, 0:1]),
          wait=[("dv", em.val("dv"))], inc="dv")
    em.op("tensor", lambda e: e.transpose(p7.ap().bitcast(BF16)[0:1, 0:4], small_bf[0:4, :],
                                          ident[0:4, 0:4]), wait=[("dv", em.val("dv"))], inc="pe")
    em.op("vector", lambda e: e.tensor_copy(rowscr[:, 0:4], p7.ap().bitcast(BF16)[0:1, 0:4]),
          wait=[("pe", em.val("pe"))], inc="dv")
    em.op("vector", lambda e: e.reduce_max(cstab[:, 0:1], rowscr[:, 0:4], X_AX),
          wait=[("dv", em.val("dv"))], inc="dv")

    if dbg == "kv":
        em.op("gpsimd", lambda e: e.dma_start(out=ap["dbg_kT"], in_=kT[:]),
              wait=[("dv", em.val("dv"))], inc="st", inc_by=16)
        em.op("gpsimd", lambda e: e.dma_start(out=ap["dbg_vaug"], in_=vaug[:]),
              inc="st", inc_by=16)
        em.op("gpsimd", lambda e: e.dma_start(out=ap["dbg_cstab"], in_=cstab[:]),
              inc="st", inc_by=16)
        em.op("gpsimd", lambda e: e.nop(), wait=[("st", em.val("st"))])
        em.finalize()
        return nc

    for (c0, cn) in CHUNKS:
        t0 = c0 // 128
        for ti in range(cn // 128):
            t = t0 + ti
            emit_tile_ln_t(em, x[:, t, :], xlnT_t[:, :, :], lnsc, pA, ident)
            for dc in range(8):
                for h in range(8):
                    w0 = [("dv", em.val("dv")), ("ac", em.val("ac"))] if (h == 0) else ()
                    em.op("tensor", lambda e, dc=dc, h=h: e.matmul(
                        p6[:, 0:128], wq[:, h, dc * 128:(dc + 1) * 128], xlnT_t[:, h, :],
                        start=(h == 0), stop=(h == 7)), wait=w0,
                        inc=("pe" if h == 7 else None))
                pe_now = em.val("pe")
                em.op("vector", lambda e, dc=dc, ti=ti: e.tensor_copy(
                    qch[:, dc, ti * 128:(ti + 1) * 128], p6[:, 0:128]),
                    wait=[("pe", pe_now)], inc="dv")
                em.op("scalar", lambda e: e.activation(sqs[:, 0:128], p6[:, 0:128], AF.Square),
                      wait=[("pe", pe_now), ("dv", em.val("dv"))], inc="ac")
                em.op("tensor", lambda e, dc=dc, ti=ti: e.matmul(
                    p7[0:16, ti * 128:(ti + 1) * 128], hselq[:, dc, :], sqs[:, 0:128],
                    start=(dc == 0), stop=(dc == 7)),
                    wait=[("ac", em.val("ac")), ("dv", em.val("dv"))],
                    inc=("pe" if dc == 7 else None))
        em.op("vector", lambda e, cn=cn: e.reduce_max(qn_f[:], p7[0:16, 0:cn], X_AX),
              wait=[("pe", em.val("pe")), ("dv", em.val("dv"))], inc="dv")
        em.op("vector", lambda e: e.tensor_copy(small_bf[:], qn_f[:]),
              wait=[("dv", em.val("dv"))], inc="dv")
        em.op("tensor", lambda e: e.transpose(p7.ap().bitcast(BF16)[0:1, 0:16], small_bf[:],
                                              ident[0:16, 0:16]), wait=[("dv", em.val("dv"))], inc="pe")
        em.op("vector", lambda e: e.tensor_copy(rowscr[:], p7.ap().bitcast(BF16)[0:1, 0:16]),
              wait=[("pe", em.val("pe")), ("dv", em.val("dv"))], inc="dv")
        em.op("vector", lambda e: e.reduce_max(cstab[:, 1:2], rowscr[:], X_AX),
              wait=[("dv", em.val("dv"))], inc="dv")
        em.op("vector", lambda e: e.tensor_tensor(cstab[:, 2:3], cstab[:, 0:1], cstab[:, 1:2],
                                                  ALU.mult), wait=[("dv", em.val("dv"))],
              inc="dv")
        em.op("scalar", lambda e: e.activation(cstab[:, 3:4], cstab[:, 2:3], AF.Sqrt),
              wait=[("dv", em.val("dv"))], inc="ac")
        em.op("vector", lambda e: e.tensor_copy(cbf[:], cstab[:, 3:4]),
              wait=[("ac", em.val("ac")), ("dv", em.val("dv"))], inc="dv")
        em.op("tensor", lambda e: e.matmul(p7[:, 0:1], ones1_128[:], cbf[:], start=True, stop=True),
              wait=[("dv", em.val("dv"))], inc="pe")
        em.op("scalar", lambda e: e.activation(biasc[:], p7[:, 0:1], AF.Copy, scale=-0.125),
              wait=[("pe", em.val("pe"))], inc="ac")

        for pi, (hl, hh) in enumerate(HPAIRS):
            gl, gh = hl // 4, hh // 4
            exp_ac = {}
            av_pe = {}
            epi_dv = em.val("dv")
            epi_ac = em.val("ac")
            for kt in range(NFT):
                par = kt % 2
                pt = pA if par == 0 else pB
                w0 = [("ac", max(exp_ac.get(kt - 2, 0), epi_ac if kt == 0 else 0))]
                if kt == 0:
                    w0.append(("dv", epi_dv))
                em.op("tensor", lambda e, pt=pt, pi=pi, gl=gl, kt=kt, cn=cn: e.matmul(
                    pt[:, 0:cn],
                    kT[0:64, gl // 2, kt * 128:(kt + 1) * 128],
                    qch[0:64, pi, 0:cn], start=True, stop=True, tile_position=(0, 0)),
                    wait=w0)
                em.op("tensor", lambda e, pt=pt, pi=pi, gh=gh, kt=kt, cn=cn: e.matmul(
                    pt[:, 512:512 + cn],
                    kT[64:128, gh // 2, kt * 128:(kt + 1) * 128],
                    qch[64:128, pi, 0:cn], start=True, stop=True, tile_position=(64, 0)),
                    inc="pe")
                em.op("scalar", lambda e, pt=pt, par=par: e.activation(
                    pbuf[:, par, :], pt[:], AF.Exp, bias=biasc[:], scale=0.125),
                    wait=[("pe", em.val("pe"))], inc="ac")
                exp_ac[kt] = em.val("ac")
                em.op("tensor", lambda e, kt=kt, gl=gl, par=par, cn=cn: e.matmul(
                    p4[0:65, 0:cn], vaug[:, kt, gl, :], pbuf[:, par, 0:cn],
                    start=(kt == 0), stop=(kt == NFT - 1)),
                    wait=[("ac", exp_ac[kt])])
                em.op("tensor", lambda e, kt=kt, gh=gh, par=par, cn=cn: e.matmul(
                    p5[0:65, 0:cn], vaug[:, kt, gh, :], pbuf[:, par, 512:512 + cn],
                    start=(kt == 0), stop=(kt == NFT - 1)), inc="pe")
                av_pe[kt] = em.val("pe")
            av_done = em.val("pe")
            for k, h in [(0, hl), (1, hh)]:
                avp = p4 if k == 0 else p5
                em.op("vector", lambda e, avp=avp, cn=cn: e.reciprocal(
                    rrow[:, 0:cn], avp[64:65, 0:cn]),
                    wait=[("pe", av_done)], inc="dv")
                em.op("tensor", lambda e, cn=cn: e.matmul(
                    pB[0:64, 0:cn], ones1_64[:], rrow[:, 0:cn], start=True, stop=True),
                    wait=[("dv", em.val("dv")), ("ac", max(exp_ac[NFT - 1], em.val("ac")))],
                    inc="pe")
                em.op("scalar", lambda e, avp=avp, cn=cn: e.activation(
                    otmp[:, 0:cn], avp[0:64, 0:cn], AF.Copy),
                    wait=[("dv", em.val("dv"))], inc="ac")
                em.op("vector", lambda e, k=k, pi=pi, cn=cn: e.tensor_tensor(
                    oT[64 * k:64 * k + 64, pi, 0:cn],
                    otmp[:, 0:cn], pB[0:64, 0:cn], ALU.mult),
                    wait=[("pe", em.val("pe")), ("ac", em.val("ac"))], inc="dv")

        em.op("tensor", lambda e: e.nop(), wait=[("dv", em.val("dv"))])
        dv_evac = {0: 0, 1: 0}
        for ti in range(cn // 128):
            t = t0 + ti
            for dh in range(2):
                par = (ti * 2 + dh) % 2
                pp = p6 if par == 0 else p7
                for dc in range(8):
                    w0 = [("dv", dv_evac[par])] if dc == 0 else ()
                    em.op("tensor", lambda e, ti=ti, dh=dh, dc=dc, pp=pp: e.matmul(
                        pp[:, 0:512], oT[:, dc, ti * 128:(ti + 1) * 128],
                        wo[:, dc, dh * 512:(dh + 1) * 512], start=(dc == 0), stop=(dc == 7)),
                        wait=w0, inc=("pe" if dc == 7 else None))
                em.op("vector", lambda e, t=t, dh=dh, pp=pp: e.tensor_tensor(
                    x[:, t, dh * 512:(dh + 1) * 512], x[:, t, dh * 512:(dh + 1) * 512],
                    pp[:, 0:512], ALU.add), wait=[("pe", em.val("pe"))], inc="dv")
                dv_evac[par] = em.val("dv")

    if dbg == "x1":
        em.op("gpsimd", lambda e: e.dma_start(out=ap["dbg_x1"], in_=x[:]),
              wait=[("dv", em.val("dv"))], inc="st", inc_by=16)
        em.op("gpsimd", lambda e: e.nop(), wait=[("st", em.val("st"))])
        em.finalize()
        return nc

    em.barrier()
    nc.sbuf_base = markA
    xlnT = A("xlnT", [128, 8, NCOL], BF16)
    markF = nc.sbuf_base
    moebufs = {
        "pA": pA, "pB": pB, "p6": p6,
        "gateT": A("gateT_s", [128, 8, 32], BF16),
        "bias_bc": A("bias_bc_s", [128, 32], F32),
        "comb": A("comb", [128, NT, 32], F32),
        "rsc": A("rsc", [128, 5, 32], F32),
        "gub": [A("gub0", [128, 8, 1024], BF16), A("gub1", [128, 8, 1024], BF16)],
        "dnb": [A("dnb0", [128, 4, 1024], BF16), A("dnb1", [128, 4, 1024], BF16)],
        "hprime": [A("hprime0", [128, 4, NCOL], BF16), A("hprime1", [128, 4, NCOL], BF16)],
        "sg": A("sg", [128, 512], F32),
    }
    dma("gpsimd", moebufs["gateT"][:], ap["gateT"], "ldw")
    dma("sync", moebufs["bias_bc"][:], ap["bias_bc"])
    ldm = (em.val("ldw"), em.val("ldx"))
    for t in range(NT):
        emit_tile_ln_t(em, x[:, t, :], xlnT[:, :, t * 128:(t + 1) * 128], lnsc, pA, ident)
    em.op("tensor", lambda e: e.nop(), wait=[("dv", em.val("dv")), ("ldw", ldm[0]),
                                             ("ldx", ldm[1])])
    em.op("vector", lambda e: e.nop(), wait=[("ldx", ldm[1])])
    emit_moe(em, ap, x, xlnT, moebufs, n_exp=n_exp, do_shared=do_shared)
    if dbg == "x2":
        em.op("gpsimd", lambda e: e.dma_start(out=ap["x2_out"], in_=x[:]),
              wait=[("dv", em.val("dv"))], inc="st", inc_by=16)
        em.op("gpsimd", lambda e: e.dma_start(out=ap["dbg_comb"], in_=moebufs["comb"][:]),
              inc="st", inc_by=16)
        em.op("gpsimd", lambda e: e.dma_start(out=ap["dbg_rsc"], in_=moebufs["rsc"][:]),
              inc="st", inc_by=16)
        em.op("gpsimd", lambda e: e.nop(), wait=[("st", em.val("st"))])
        em.finalize()
        return nc

    em.barrier()
    nc.sbuf_base = markF
    for t in range(NT):
        emit_tile_ln_t(em, x[:, t, :], xlnT[:, :, t * 128:(t + 1) * 128], lnsc, pA, ident)
    em.op("sync", lambda e: e.dma_start(out=ap["x2_out"], in_=x[:]),
          wait=[("dv", em.val("dv"))], inc="st", inc_by=16)
    em.op("gpsimd", lambda e: e.nop(), wait=[("st", em.val("st"))])
    em.barrier()
    nc.sbuf_top = top0

    qkv = {"q": A("qb_s", [128, 8, 2048], BF16), "k": A("kb_s", [128, 8, 2048], BF16),
           "v": A("vb_s", [128, 8, 2048], BF16)}
    markQ = nc.sbuf_base
    wb = {"q": A("wqb", [128, 8, 1024], BF16), "k": A("wkb", [128, 8, 1024], BF16),
          "v": A("wvb", [128, 8, 1024], BF16)}
    cws = A("cws", [128, 3, 8, 4], F32)
    pre = A("pre", [128, NCOL], F32)
    cacc = A("cacc", [128, 2048], F32)
    sqs2 = A("sqs2", [128, 2, 512], BF16)
    dtmp2 = A("dtmp2", [1, 512], F32)
    rrow2 = A("rrow2", [1, 512], BF16)
    ones128_1b = A("ones128_1b", [128, 1], BF16)
    ones1_128b = A("ones1_128b", [1, 128], BF16)
    epsn = A("epsn", [1, 2], F32)
    em.op("vector", lambda e: e.memset(ones128_1b[:], 1.0))
    em.op("vector", lambda e: e.memset(ones1_128b[:], 1.0))
    em.op("vector", lambda e: e.memset(epsn[:, 0:1], 256e-6))
    em.op("vector", lambda e: e.memset(epsn[:, 1:2], 1e-6), inc="dv")
    for nm, t in [("wq_b", wb["q"]), ("wk_b", wb["k"]), ("wv_b", wb["v"])]:
        dma("gpsimd", t[:], ap[nm], "ldw")
    em.op("sync", lambda e: e.dma_start(out=cws[:], in_=ap["cw"].rearrange("t p c k -> p t c k")),
          inc="ldx", inc_by=16)
    ldb = (em.val("ldw"), em.val("ldx"))
    em.op("tensor", lambda e: e.nop(), wait=[("ldw", ldb[0]), ("ldx", ldb[1])])
    em.op("vector", lambda e: e.nop(), wait=[("ldx", ldb[1])])

    if dbg == "qkv" and q3 == 1:
        em.op("sync", lambda e: e.nop(), wait=[("st", em.val("st"))])
        em.op("gpsimd", lambda e: e.nop(), wait=[("st", em.val("st"))])
        em.finalize()
        return nc

    pp_free = {0: 0, 1: 0}
    for wi, nm in enumerate(("q", "k", "v")):
        if dbg == "qkv" and q3 == 2 and wi > 0:
            break
        for dc in range(8):
            for ci, (c0, cn) in enumerate(CHUNKS):
                pi2 = ci % 2
                pp = p6 if pi2 == 0 else p7
                for h in range(8):
                    w0 = [("dv", pp_free[pi2])] if h == 0 else ()
                    em.op("tensor", lambda e, nm=nm, h=h, dc=dc, c0=c0, cn=cn, pp=pp: e.matmul(
                        pp[:, 0:cn], wb[nm][:, h, dc * 128:(dc + 1) * 128],
                        xlnT[:, h, c0:c0 + cn], start=(h == 0), stop=(h == 7)),
                        wait=w0, inc=("pe" if h == 7 else None))
                em.op("vector", lambda e, c0=c0, cn=cn, pp=pp: e.tensor_copy(
                    pre[:, c0:c0 + cn], pp[:, 0:cn]), wait=[("pe", em.val("pe"))], inc="dv")
                pp_free[pi2] = em.val("dv")
            em.op("vector", lambda e, wi=wi, dc=dc: e.tensor_scalar(
                cacc[:], pre[:, 125:125 + 2048], cws[:, wi, dc, 0:1], None, ALU.mult),
                wait=[("dv", em.val("dv")), ("ac", em.val("ac"))], inc="dv")
            for tap in range(1, 4):
                em.op("vector", lambda e, wi=wi, dc=dc, tap=tap: e.scalar_tensor_tensor(
                    cacc[:], pre[:, 125 + tap:125 + tap + 2048], cws[:, wi, dc, tap:tap + 1],
                    cacc[:], ALU.mult, ALU.add), wait=[("dv", em.val("dv"))], inc="dv")
            em.op("scalar", lambda e, nm=nm, dc=dc: e.activation(
                qkv[nm][:, dc, :], cacc[:], AF.Silu), wait=[("dv", em.val("dv"))], inc="ac")
        em.op("vector", lambda e: e.nop(), wait=[("ac", em.val("ac"))])

    if dbg == "qkv" and q3 == 2:
        em.op("sync", lambda e: e.nop(), wait=[("st", em.val("st"))])
        em.op("gpsimd", lambda e: e.nop(), wait=[("st", em.val("st"))])
        em.finalize()
        return nc

    if dbg == "qkv" and q3 == 3:
        em.op("sync", lambda e: e.nop(), wait=[("st", em.val("st"))])
        em.op("gpsimd", lambda e: e.nop(), wait=[("st", em.val("st"))])
        em.finalize()
        return nc

    for wi2, (nm, scale) in enumerate((("q", 256.0), ("k", 1.0))):
        for h in range(H_B):
            for tc in range(4):
                c0 = tc * 512
                for ci, c2 in enumerate((2 * h, 2 * h + 1)):
                    em.op("scalar", lambda e, nm=nm, c2=c2, c0=c0, ci=ci: e.activation(
                        sqs2[:, ci, :], qkv[nm][:, c2, c0:c0 + 512], AF.Square),
                        wait=[("pe", em.val("pe"))], inc="ac")
                    em.op("tensor", lambda e, ci=ci: e.matmul(
                        p6[0:1, 0:512], ones128_1b[:], sqs2[:, ci, :],
                        start=(ci == 0), stop=(ci == 1)),
                        wait=[("ac", em.val("ac"))], inc=("pe" if ci == 1 else None))
                em.op("scalar", lambda e, scale=scale, wi2=wi2: e.activation(
                    dtmp2[:, 0:512], p6[0:1, 0:512], AF.Sqrt, scale=scale,
                    bias=epsn[:, wi2:wi2 + 1]),
                    wait=[("pe", em.val("pe"))], inc="ac")
                em.op("vector", lambda e: e.reciprocal(rrow2[:, 0:512], dtmp2[:, 0:512]),
                      wait=[("ac", em.val("ac"))], inc="dv")
                em.op("tensor", lambda e: e.matmul(p7[:, 0:512], ones1_128b[:], rrow2[:, 0:512],
                                                   start=True, stop=True),
                      wait=[("dv", em.val("dv"))], inc="pe")
                pe_now = em.val("pe")
                for ci, c2 in enumerate((2 * h, 2 * h + 1)):
                    em.op("vector", lambda e, nm=nm, c2=c2, c0=c0: e.tensor_tensor(
                        qkv[nm][:, c2, c0:c0 + 512], qkv[nm][:, c2, c0:c0 + 512],
                        p7[:, 0:512], ALU.mult), wait=[("pe", pe_now)],
                        inc=("dv" if ci == 1 else None))
                em.op("tensor", lambda e: e.nop(), wait=[("dv", em.val("dv"))])
    em.op("sync", lambda e: e.dma_start(out=ap["spill_Q"], in_=qkv["q"][:]),
          wait=[("dv", em.val("dv"))], inc="st", inc_by=16)
    if dbg == "qkv":
        em.op("sync", lambda e: e.nop(), wait=[("st", em.val("st"))])
        em.op("gpsimd", lambda e: e.nop(), wait=[("st", em.val("st"))])
        em.finalize()
        return nc

    em.barrier()
    nc.sbuf_base = markQ
    masks = {"stril": A("stril_s", [128, 128], BF16), "striu": A("striu_s", [128, 128], BF16),
             "triu_i": A("triu_i_s", [128, 128], BF16)}
    i256s = A("i256_s", [128, 2, 256], BF16)
    for nm in ("stril", "striu", "triu_i"):
        dma("sync", masks[nm][:], ap[nm])
    dma("sync", i256s[:], ap["i256"])
    ld_msk = em.val("ldx")
    em.op("vector", lambda e: e.nop(), wait=[("ldx", ld_msk)])
    em.op("tensor", lambda e: e.nop(), wait=[("ldx", ld_msk)])

    LTs = A("LTs", [128, 7, 128], BF16)
    Lcur = A("Lcur", [128, 2, 128], BF16)
    Xb = [A("Xb0", [128, 512], BF16), A("Xb1", [128, 512], BF16)]
    Ktok = A("Ktok", [128, 256], BF16)
    WTs = A("WTs", [128, 2, 128], BF16)
    TTs = A("TTs", [128, 128], BF16)
    AjT = A("AjT", [128, 2, 256], BF16)
    Pst = [[A(f"P{h}_{j}", [128, 2, 256], BF16) for j in range(2)] for h in range(H_B)]
    PTst = [[A(f"PT{h}_{j}", [128, 2, 256], BF16) for j in range(2)] for h in range(H_B)]
    Bst = [[A(f"B{h}_{j}", [128, 2, 256], BF16) for j in range(2)] for h in range(H_B)]
    for h in range(H_B):
        em.op("vector", lambda e, h=h: e.tensor_copy(Pst[h][0][:], i256s[:]))
        em.op("vector", lambda e, h=h: e.tensor_copy(PTst[h][0][:], i256s[:]))
        em.op("vector", lambda e, h=h: e.memset(Bst[h][0][:], 0.0),
              inc=("dv" if h == 3 else None))

    for j in range(nch):
        jcols = slice(128 * j, 128 * (j + 1))
        for h in range(H_B):
            pj = j % 2
            ptv = pA.ap().bitcast(BF16)
            em.op("tensor", lambda e: e.nop(),
                  wait=[("dv", em.val("dv")), ("st", em.val("st"))])
            for ci, c2 in enumerate((2 * h, 2 * h + 1)):
                em.op("tensor", lambda e, c2=c2, ci=ci, jcols=jcols: e.transpose(
                    ptv[:, ci * 128:(ci + 1) * 128], qkv["v"][:, c2, jcols], ident[:]))
                em.op("tensor", lambda e, c2=c2, ci=ci, jcols=jcols: e.transpose(
                    ptv[:, 256 + ci * 128:256 + (ci + 1) * 128], qkv["k"][:, c2, jcols],
                    ident[:]), inc=("pe" if ci == 1 else None))
            pe_now = em.val("pe")
            em.op("vector", lambda e: e.tensor_copy(Xb[0][:], ptv[:, 0:512]),
                  wait=[("pe", pe_now)])
            em.op("vector", lambda e: e.tensor_copy(Ktok[:], ptv[:, 256:512]), inc="dv")
            em.op("tensor", lambda e, h=h, jcols=jcols: e.matmul(
                pB[:, 0:128], qkv["k"][:, 2 * h, jcols], qkv["k"][:, 2 * h, jcols],
                start=True, stop=False), wait=[("dv", em.val("dv"))])
            em.op("tensor", lambda e, h=h, jcols=jcols: e.matmul(
                pB[:, 0:128], qkv["k"][:, 2 * h + 1, jcols], qkv["k"][:, 2 * h + 1, jcols],
                start=False, stop=True), inc="pe")
            em.op("vector", lambda e: e.tensor_tensor(Lcur[:, 0, :], pB[:, 0:128],
                                                      masks["stril"][:], ALU.mult),
                  wait=[("pe", em.val("pe"))])
            em.op("vector", lambda e: e.tensor_tensor(LTs[:, 0, :], pB[:, 0:128],
                                                      masks["striu"][:], ALU.mult), inc="dv")
            for lev in range(1, 7):
                lp = Lcur[:, (lev - 1) % 2, :]
                lpn = Lcur[:, lev % 2, :]
                em.op("tensor", lambda e, lp=lp, lev=lev: e.matmul(
                    pB[:, 0:128], LTs[:, lev - 1, :], lp, start=True, stop=True),
                    wait=[("dv", em.val("dv"))], inc="pe")
                em.op("tensor", lambda e, lp=lp, lev=lev: e.matmul(
                    pB[:, 128:256], lp, LTs[:, lev - 1, :], start=True, stop=True), inc="pe")
                em.op("vector", lambda e, lpn=lpn: e.tensor_copy(lpn, pB[:, 0:128]),
                      wait=[("pe", em.val("pe"))])
                em.op("vector", lambda e, lev=lev: e.tensor_copy(LTs[:, lev, :], pB[:, 128:256]),
                      inc="dv")
            cur = 0
            for lev in [6, 5, 4, 3, 2, 1, 0]:
                sub = (lev == 0)
                em.op("tensor", lambda e, lev=lev, cur=cur: e.matmul(
                    pB[:, 512:1024], LTs[:, lev, :], Xb[cur][:], start=True, stop=True),
                    wait=[("dv", em.val("dv"))], inc="pe")
                em.op("vector", lambda e, cur=cur, sub=sub: e.tensor_tensor(
                    Xb[cur ^ 1][:], Xb[cur][:], pB[:, 512:1024],
                    ALU.subtract if sub else ALU.add),
                    wait=[("pe", em.val("pe")), ("dv", em.val("dv"))], inc="dv")
                cur ^= 1
            Xf = Xb[cur]
            em.op("sync", lambda e, h=h, j=j, Xf=Xf: e.dma_start(
                out=ap["spill_U0"][h, j], in_=Xf[:, 0:256]),
                wait=[("dv", em.val("dv"))], inc="st", inc_by=16)
            em.op("sync", lambda e, h=h, j=j: e.dma_start(
                out=ap["spill_K"][h, j], in_=Ktok[:]), inc="st", inc_by=16)
            em.op("tensor", lambda e, Xf=Xf: e.transpose(ptv[:, 0:128], Xf[:, 256:384],
                                                         ident[:]), wait=[("dv", em.val("dv"))])
            em.op("tensor", lambda e, Xf=Xf: e.transpose(ptv[:, 128:256], Xf[:, 384:512],
                                                         ident[:]), inc="pe")
            em.op("vector", lambda e: e.tensor_copy(
                WTs[:], ptv[:, 0:256].rearrange("p (c w) -> p c w", c=2)),
                  wait=[("pe", em.val("pe"))], inc="dv")
            em.op("sync", lambda e, h=h, j=j: e.dma_start(out=ap["spill_WT"][h, j], in_=WTs[:]),
                  wait=[("dv", em.val("dv"))], inc="st", inc_by=16)
            em.op("tensor", lambda e, h=h, jcols=jcols: e.matmul(
                pB[:, 0:128], qkv["k"][:, 2 * h, jcols], qkv["q"][:, 2 * h, jcols],
                start=True, stop=False), wait=[("dv", em.val("dv"))])
            em.op("tensor", lambda e, h=h, jcols=jcols: e.matmul(
                pB[:, 0:128], qkv["k"][:, 2 * h + 1, jcols], qkv["q"][:, 2 * h + 1, jcols],
                start=False, stop=True), inc="pe")
            em.op("vector", lambda e: e.tensor_tensor(TTs[:], pB[:, 0:128], masks["triu_i"][:],
                                                      ALU.mult), wait=[("pe", em.val("pe"))],
                  inc="dv")
            em.op("sync", lambda e, h=h, j=j: e.dma_start(out=ap["spill_TT"][h, j], in_=TTs[:]),
                  wait=[("dv", em.val("dv"))], inc="st", inc_by=16)
            for c2 in range(2):
                em.op("tensor", lambda e, Xf=Xf, c2=c2: e.matmul(
                    pB[:, 0:256], Xf[:, 256 + c2 * 128:256 + (c2 + 1) * 128], Ktok[:],
                    start=True, stop=True), wait=[("dv", em.val("dv"))], inc="pe")
                em.op("vector", lambda e, c2=c2: e.tensor_tensor(
                    AjT[:, c2, :], i256s[:, c2, :], pB[:, 0:256], ALU.subtract),
                    wait=[("pe", em.val("pe"))], inc="dv")
            Po, Pn = Pst[h][pj], Pst[h][pj ^ 1]
            Bo, Bn = Bst[h][pj], Bst[h][pj ^ 1]
            PTn = PTst[h][pj ^ 1]
            for oc in range(2):
                for kc in range(2):
                    em.op("tensor", lambda e, oc=oc, kc=kc, Po=Po: e.matmul(
                        pB[:, 512:768], AjT[:, kc, oc * 128:(oc + 1) * 128], Po[:, kc, :],
                        start=(kc == 0), stop=(kc == 1)),
                        wait=([("dv", em.val("dv"))] if kc == 0 else ()),
                        inc=("pe" if kc == 1 else None))
                em.op("vector", lambda e, oc=oc, Pn=Pn: e.tensor_copy(Pn[:, oc, :],
                                                                      pB[:, 512:768]),
                      wait=[("pe", em.val("pe"))], inc="dv")
                for kc in range(2):
                    em.op("tensor", lambda e, oc=oc, kc=kc, Po=Po: e.matmul(
                        pB[:, 768:1024], Po[:, kc, oc * 128:(oc + 1) * 128], AjT[:, kc, :],
                        start=(kc == 0), stop=(kc == 1)),
                        wait=([("dv", em.val("dv"))] if kc == 0 else ()),
                        inc=("pe" if kc == 1 else None))
                em.op("vector", lambda e, oc=oc, PTn=PTn: e.tensor_copy(PTn[:, oc, :],
                                                                        pB[:, 768:1024]),
                      wait=[("pe", em.val("pe"))], inc="dv")
                for kc in range(2):
                    em.op("tensor", lambda e, oc=oc, kc=kc, Bo=Bo: e.matmul(
                        pB[:, 512:768], AjT[:, kc, oc * 128:(oc + 1) * 128], Bo[:, kc, :],
                        start=(kc == 0), stop=False),
                        wait=([("dv", em.val("dv"))] if kc == 0 else ()))
                em.op("tensor", lambda e, oc=oc, Xf=Xf: e.matmul(
                    pB[:, 512:768], Ktok[:, oc * 128:(oc + 1) * 128], Xf[:, 0:256],
                    start=False, stop=True), inc="pe")
                em.op("vector", lambda e, oc=oc, Bn=Bn: e.tensor_copy(Bn[:, oc, :],
                                                                      pB[:, 512:768]),
                      wait=[("pe", em.val("pe"))], inc="dv")
    fin = NCH % 2
    for h in range(H_B):
        em.op("sync", lambda e, h=h: e.dma_start(out=ap["maps_PT"][h], in_=PTst[h][fin][:]),
              wait=[("dv", em.val("dv"))], inc="st", inc_by=16)
        em.op("sync", lambda e, h=h: e.dma_start(out=ap["maps_B"][h], in_=Bst[h][fin][:]),
              inc="st", inc_by=16)
    em.op("sync", lambda e: e.nop(), wait=[("st", em.val("st"))])
    em.op("gpsimd", lambda e: e.nop(), wait=[("st", em.val("st"))])
    em.finalize()
    return nc



def build_launch_B():
    nc = bass.Bass("TRN2", target_bir_lowering=False, detect_race_conditions=False)
    ap = {}

    def IN(name, shape, dt=F32):
        ap[name] = nc.dram_tensor(name, shape, dt, kind="ExternalInput").ap()

    IN("x2_in", [128, NT, D]); IN("q_in", [128, 8, 2048], BF16)
    IN("u0_in", [H_B, NCH, 128, 256], BF16); IN("wt_in", [H_B, NCH, 128, 2, 128], BF16)
    IN("k_in", [H_B, NCH, 128, 256], BF16); IN("tt_in", [H_B, NCH, 128, 128], BF16)
    IN("mapPT_in", [3, H_B, 128, 2, 256], BF16); IN("mapB_in", [3, H_B, 128, 2, 256], BF16)
    IN("gates", [128, 3])
    IN("guT", [NE, 128, 8, 1024]); IN("dnT", [NE, 128, 4, 1024])
    IN("sgT", [128, 8, 1024]); IN("suT", [128, 8, 1024]); IN("sdT", [128, 8, 1024])
    IN("gateT", [128, 8, 32]); IN("bias_bc", [128, 32])
    IN("wo_b", [128, 8, 1024]); IN("wrep", [128, 8])
    IN("ident", [128, 128], BF16); IN("i256", [128, 2, 256], BF16)
    out = nc.dram_tensor("out", [128, 16, D], F32, kind="ExternalOutput").ap()

    em = Em(nc)
    A = nc.alloc_sbuf_tensor
    PS = nc.alloc_psum_tensor

    x = A("x", [128, NT, D], F32, side="right")
    ident = A("s_ident", [128, 128], BF16)
    lnsc = {"m": A("ln_m", [128, 1], F32), "s2": A("ln_s2", [128, 1], F32),
            "t2": A("ln_t2", [128, 1], F32),
            "xln": A("ln_xln", [128, 1, D], BF16),
            "eps": A("ln_eps", [128, 1], F32)}
    markE = nc.sbuf_base
    qb = A("qb", [128, 8, 2048], BF16)
    oT = A("oT_b", [128, 8, NCOL], BF16)
    markC = nc.sbuf_base
    pA = PS("pA", [128, 1024], F32)
    pB = PS("pB", [128, 1024], F32)
    p6 = PS("p6", [128, 512], F32)
    p7 = PS("p7", [128, 512], F32)

    def dma(eng, dst, src, sem="ldx"):
        return em.op(eng, lambda e, dst=dst, src=src: e.dma_start(out=dst, in_=src),
                     inc=sem, inc_by=16)

    dma("sync", x[:], ap["x2_in"])
    dma("sync", ident[:], ap["ident"])
    dma("sync", qb[:], ap["q_in"])
    mp = A("mpPT", [128, 3, H_B, 2, 256], BF16)
    mb = A("mpB", [128, 3, H_B, 2, 256], BF16)
    gsb = A("gsb", [128, 3], F32)
    i256s = A("i256_s", [128, 2, 256], BF16)
    Sf = A("Sf", [128, H_B, 2, 256], F32)
    Sb = A("Sb", [128, H_B, 2, 256], BF16)
    t1 = A("t1", [128, 2, 256], F32)
    spb = [{"u0": A(f"u0_{s}", [128, H_B, 256], BF16),
            "wt": A(f"wt_{s}", [128, H_B, 2, 128], BF16),
            "k": A(f"k_{s}", [128, H_B, 256], BF16),
            "tt": A(f"tt_{s}", [128, H_B, 128], BF16)} for s in range(2)]
    Y = A("Y", [128, 256], BF16)
    onorm = A("onorm_sc", [128, 258], F32)
    obuf = A("obuf", [128, 256], BF16)
    dma("sync", mp[:], ap["mapPT_in"].rearrange("r h p c w -> p r h c w"))
    dma("sync", mb[:], ap["mapB_in"].rearrange("r h p c w -> p r h c w"))
    dma("sync", gsb[:], ap["gates"])
    dma("sync", i256s[:], ap["i256"])
    ld0 = em.val("ldx")
    for eng in ("tensor", "vector", "scalar"):
        em.op(eng, lambda e: e.nop(), wait=[("ldx", ld0)])

    em.op("vector", lambda e: e.memset(lnsc["eps"][:], 1e-5))
    em.op("vector", lambda e: e.memset(Sf[:], 0.0))
    em.op("vector", lambda e: e.memset(Sb[:], 0.0))
    em.op("vector", lambda e: e.memset(oT[:, :, 0:128], 0.0), inc="dv")
    for r in range(3):
        for h in range(H_B):
            for oc in range(2):
                for kc in range(2):
                    em.op("tensor", lambda e, r=r, h=h, oc=oc, kc=kc: e.matmul(
                        pA[:, 0:256], mp[:, r, h, kc, oc * 128:(oc + 1) * 128],
                        Sb[:, h, kc, :], start=(kc == 0), stop=(kc == 1)),
                        wait=([("dv", em.val("dv"))] if kc == 0 else ()),
                        inc=("pe" if kc == 1 else None))
                em.op("vector", lambda e, r=r, h=h, oc=oc: e.tensor_tensor(
                    t1[:, oc, :], mb[:, r, h, oc, :], pA[:, 0:256], ALU.add),
                    wait=[("pe", em.val("pe"))], inc="dv")
            for oc in range(2):
                em.op("vector", lambda e, h=h, oc=oc: e.tensor_tensor(
                    t1[:, oc, :], t1[:, oc, :], Sf[:, h, oc, :], ALU.subtract),
                    wait=[("dv", em.val("dv"))], inc="dv")
                em.op("vector", lambda e, r=r, h=h, oc=oc: e.scalar_tensor_tensor(
                    Sf[:, h, oc, :], t1[:, oc, :], gsb[:, r:r + 1], Sf[:, h, oc, :],
                    ALU.mult, ALU.add), wait=[("dv", em.val("dv"))], inc="dv")
        for h in range(H_B):
            em.op("vector", lambda e, h=h: e.tensor_copy(Sb[:, h, :, :], Sf[:, h, :, :]),
                  wait=[("dv", em.val("dv"))], inc="dv")

    ld_done = {}
    chain_pe = {}
    for j in range(NCH + 1):
        s = j % 2
        if j < NCH:
            for h in range(H_B):
                w0 = [("pe", chain_pe.get(j - 2, 0))] if h == 0 else ()
                em.op("sync", lambda e, j=j, h=h, s=s: e.dma_start(
                    out=spb[s]["u0"][:, h, :], in_=ap["u0_in"][h, j]), wait=w0,
                    inc="ldx", inc_by=16)
                em.op("sync", lambda e, j=j, h=h, s=s: e.dma_start(
                    out=spb[s]["wt"][:, h, :, :], in_=ap["wt_in"][h, j]), inc="ldx", inc_by=16)
                em.op("sync", lambda e, j=j, h=h, s=s: e.dma_start(
                    out=spb[s]["k"][:, h, :], in_=ap["k_in"][h, j]), inc="ldx", inc_by=16)
                em.op("sync", lambda e, j=j, h=h, s=s: e.dma_start(
                    out=spb[s]["tt"][:, h, :], in_=ap["tt_in"][h, j]), inc="ldx", inc_by=16)
            ld_done[j] = em.val("ldx")
        if j == 0:
            continue
        jj = j - 1
        s = jj % 2
        col0 = 128 * jj
        for h in range(H_B):
            w0 = [("ldx", ld_done[jj]), ("dv", em.val("dv"))]
            for kc in range(2):
                em.op("tensor", lambda e, h=h, s=s, kc=kc: e.matmul(
                    pA[:, 0:256], spb[s]["wt"][:, h, kc, :], Sb[:, h, kc, :],
                    start=(kc == 0), stop=(kc == 1)), wait=(w0 if kc == 0 else ()),
                    inc=("pe" if kc == 1 else None))
            em.op("vector", lambda e, h=h, s=s: e.tensor_tensor(
                Y[:], spb[s]["u0"][:, h, :], pA[:, 0:256], ALU.subtract),
                wait=[("pe", em.val("pe"))], inc="dv")
            ydv = em.val("dv")
            for kc in range(2):
                em.op("tensor", lambda e, h=h, kc=kc, col0=col0: e.matmul(
                    pA[:, 512:768], qb[:, 2 * h + kc, col0:col0 + 128], Sb[:, h, kc, :],
                    start=(kc == 0), stop=False))
            em.op("tensor", lambda e, h=h, s=s: e.matmul(
                pA[:, 512:768], spb[s]["tt"][:, h, :], Y[:], start=False, stop=True),
                wait=[("dv", ydv)], inc="pe")
            o_pe = em.val("pe")
            for kc in range(2):
                em.op("tensor", lambda e, h=h, s=s, kc=kc: e.matmul(
                    pB[:, (0 if kc == 0 else 512):(256 if kc == 0 else 768)],
                    spb[s]["k"][:, h, kc * 128:(kc + 1) * 128], Y[:],
                    start=True, stop=True), inc=("pe" if kc == 1 else None))
            su_pe = em.val("pe")
            for kc in range(2):
                em.op("vector", lambda e, h=h, kc=kc: e.tensor_tensor(
                    Sf[:, h, kc, :], Sf[:, h, kc, :],
                    pB[:, 0:256] if kc == 0 else pB[:, 512:768], ALU.add),
                    wait=[("pe", su_pe), ("dv", em.val("dv"))], inc="dv")
            em.op("vector", lambda e, h=h: e.tensor_copy(Sb[:, h, :, :], Sf[:, h, :, :]),
                  wait=[("dv", em.val("dv"))], inc="dv")
            em.op("scalar", lambda e: e.activation(onorm[:, 0:256], pA[:, 512:768], AF.Square),
                  wait=[("pe", o_pe)], inc="ac")
            em.op("vector", lambda e: e.reduce_sum(onorm[:, 256:257], onorm[:, 0:256], X_AX),
                  wait=[("ac", em.val("ac")), ("dv", em.val("dv"))], inc="dv")
            em.op("vector", lambda e: e.tensor_scalar(onorm[:, 256:257], onorm[:, 256:257],
                                                      1.0 / 256, 1e-5, ALU.mult, ALU.add),
                  wait=[("dv", em.val("dv"))], inc="dv")
            em.op("scalar", lambda e: e.activation(onorm[:, 257:258], onorm[:, 256:257],
                                                   AF.Sqrt), wait=[("dv", em.val("dv"))],
                  inc="ac")
            em.op("vector", lambda e: e.reciprocal(onorm[:, 256:257], onorm[:, 257:258]),
                  wait=[("ac", em.val("ac")), ("dv", em.val("dv"))], inc="dv")
            em.op("vector", lambda e: e.tensor_scalar(obuf[:], pA[:, 512:768],
                                                      onorm[:, 256:257], None, ALU.mult),
                  wait=[("dv", em.val("dv"))], inc="dv")
            ptv = pB.ap().bitcast(BF16)
            for c2 in range(2):
                em.op("tensor", lambda e, c2=c2: e.transpose(
                    ptv[:, 1536 + c2 * 128:1536 + (c2 + 1) * 128],
                    obuf[:, c2 * 128:(c2 + 1) * 128], ident[:]),
                    wait=([("dv", em.val("dv"))] if c2 == 0 else ()),
                    inc=("pe" if c2 == 1 else None))
            em.op("vector", lambda e, h=h, col0=col0: e.tensor_copy(
                oT[:, 2 * h:2 * h + 2, 128 + col0:128 + col0 + 128],
                ptv[:, 1536:1792].rearrange("p (c w) -> p c w", c=2)),
                wait=[("pe", em.val("pe"))], inc="dv")
        chain_pe[jj] = em.val("pe")

    em.barrier()
    nc.sbuf_base = markC
    wob = A("wob", [128, 8, 1024], BF16)
    wreps = A("wreps", [128, 8], F32)
    dma("gpsimd", wob[:], ap["wo_b"], "ldw")
    dma("sync", wreps[:], ap["wrep"])
    em.op("vector", lambda e: e.nop(), wait=[("ldw", em.val("ldw")), ("ldx", em.val("ldx"))])
    for c in range(8):
        em.op("vector", lambda e, c=c: e.tensor_scalar(wob[:, c, :], wob[:, c, :],
                                                       wreps[:, c:c + 1], None, ALU.mult),
              inc=("dv" if c == 7 else None))
    em.op("tensor", lambda e: e.nop(), wait=[("dv", em.val("dv"))])
    dv_evac = {0: 0, 1: 0}
    for t in range(NT):
        for dh in range(2):
            par = (t * 2 + dh) % 2
            pp = p6 if par == 0 else p7
            for dc in range(8):
                w0 = [("dv", dv_evac[par])] if dc == 0 else ()
                em.op("tensor", lambda e, t=t, dh=dh, dc=dc, pp=pp: e.matmul(
                    pp[:, 0:512], oT[:, dc, t * 128:(t + 1) * 128],
                    wob[:, dc, dh * 512:(dh + 1) * 512], start=(dc == 0), stop=(dc == 7)),
                    wait=w0, inc=("pe" if dc == 7 else None))
            em.op("vector", lambda e, t=t, dh=dh, pp=pp: e.tensor_tensor(
                x[:, t, dh * 512:(dh + 1) * 512], x[:, t, dh * 512:(dh + 1) * 512],
                pp[:, 0:512], ALU.add), wait=[("pe", em.val("pe"))], inc="dv")
            dv_evac[par] = em.val("dv")

    em.barrier()
    nc.sbuf_base = markE
    xlnT = A("xlnT", [128, 8, NCOL], BF16)
    moebufs = {
        "pA": pA, "pB": pB, "p6": p6,
        "gateT": A("gateT_s", [128, 8, 32], BF16),
        "bias_bc": A("bias_bc_s", [128, 32], F32),
        "comb": A("comb", [128, NT, 32], F32),
        "rsc": A("rsc", [128, 5, 32], F32),
        "gub": [A("gub0", [128, 8, 1024], BF16), A("gub1", [128, 8, 1024], BF16)],
        "dnb": [A("dnb0", [128, 4, 1024], BF16), A("dnb1", [128, 4, 1024], BF16)],
        "hprime": [A("hprime0", [128, 4, NCOL], BF16), A("hprime1", [128, 4, NCOL], BF16)],
        "sg": A("sg", [128, 512], F32),
    }
    dma("gpsimd", moebufs["gateT"][:], ap["gateT"], "ldw")
    dma("sync", moebufs["bias_bc"][:], ap["bias_bc"])
    ldm = (em.val("ldw"), em.val("ldx"))
    for t in range(NT):
        emit_tile_ln_t(em, x[:, t, :], xlnT[:, :, t * 128:(t + 1) * 128], lnsc, pA, ident)
    em.op("tensor", lambda e: e.nop(), wait=[("dv", em.val("dv")), ("ldw", ldm[0]),
                                             ("ldx", ldm[1])])
    em.op("vector", lambda e: e.nop(), wait=[("ldx", ldm[1])])
    emit_moe(em, ap, x, xlnT, moebufs)

    em.op("gpsimd", lambda e: e.dma_start(out=out, in_=x[:, 1:NT, :]),
          wait=[("dv", em.val("dv"))], inc="st", inc_by=16)
    em.op("gpsimd", lambda e: e.nop(), wait=[("st", em.val("st"))])
    em.finalize()
    return nc



def kernel(**inputs):
    per_core = host_stage(inputs)
    nc_a = build_launch_A()
    in_maps_a = [{k: pc[k] for k in IN_NAMES_A} for pc in per_core]
    res_a = run_bass_kernel_spmd(nc_a, in_maps_a, core_ids=list(range(8)))

    nc_b = build_launch_B()
    zmap = np.zeros((H_B, 128, 2, 256), bf16)
    in_maps_b = []
    for c in range(8):
        b, r = c // 4, c % 4
        ra = res_a.results[c]
        d = {
            "x2_in": np.asarray(ra["x2_out"]),
            "q_in": np.asarray(ra["spill_Q"]),
            "u0_in": np.asarray(ra["spill_U0"]),
            "wt_in": np.asarray(ra["spill_WT"]),
            "k_in": np.asarray(ra["spill_K"]),
            "tt_in": np.asarray(ra["spill_TT"]),
            "mapPT_in": np.stack([np.asarray(res_a.results[4 * b + jr]["maps_PT"])
                                  if jr < r else zmap for jr in range(3)]),
            "mapB_in": np.stack([np.asarray(res_a.results[4 * b + jr]["maps_B"])
                                 if jr < r else zmap for jr in range(3)]),
            "gates": per_core[c]["gates"],
        }
        for k in ("guT", "dnT", "sgT", "suT", "sdT", "gateT", "bias_bc", "wo_b",
                  "wrep", "ident", "i256"):
            d[k] = per_core[c][k]
        in_maps_b.append(d)
    res_b = run_bass_kernel_spmd(nc_b, in_maps_b, core_ids=list(range(8)))

    out = np.zeros((2, 8192, 1024), np.float32)
    for c in range(8):
        b, r = c // 4, c % 4
        o = np.asarray(res_b.results[c]["out"])
        out[b, 2048 * r:2048 * (r + 1)] = o.transpose(1, 0, 2).reshape(2048, 1024)
    return out
